# version 24
# baseline (speedup 1.0000x reference)
import numpy as np
import concourse.bass as bass
import concourse.mybir as mybir
from concourse.bass_utils import run_bass_kernel_spmd

F32 = mybir.dt.float32
BF16 = mybir.dt.bfloat16
I32 = mybir.dt.int32
ALU = mybir.AluOpType
ACT = mybir.ActivationFunctionType

D = 2048
I = 5632
T = 1024
KC = D // 128
ICN = I // 128
IH = 2
ICH = ICN // IH
TH = 2
SLOT = 2816


class Build:

    def __init__(self, nc):
        self.nc = nc
        self.q = {"g": [], "y": [], "t": [], "s": [], "v": []}
        self.cnt = {"mm": 0, "sc": 0, "ev": 0, "cc": 0, "gv": 0, "wi": 0,
                    "w0": 0, "w1": 0, "w2": 0, "w3": 0,
                    "xa": 0, "xb": 0, "xc": 0, "xd": 0,
                    "xe": 0, "xf": 0, "xg": 0, "xh": 0, "xs": 0}

    def wait(self, eng, sem, val):
        if val > 0:
            self.q[eng].append(lambda e, S, sem=sem, val=val: e.wait_ge(S[sem], val))

    def op(self, eng, fn, sem=None, inc=1):
        if sem is None:
            self.q[eng].append(lambda e, S, fn=fn: fn(e, S))
            return None
        self.cnt[sem] += inc
        after = self.cnt[sem]
        self.q[eng].append(
            lambda e, S, fn=fn, sem=sem, inc=inc: fn(e, S).then_inc(S[sem], inc))
        return after


def build_nc():
    nc = bass.Bass("TRN2", target_bir_lowering=False)

    x0 = nc.dram_tensor("x0", [128, KC * T], F32, kind="ExternalInput").ap()
    wi_b = nc.dram_tensor("wi_b", [KC, 128, 2048], F32, kind="ExternalInput").ap()
    wf_b = nc.dram_tensor("wf_b", [KC, 128, 2048], F32, kind="ExternalInput").ap()
    wg_b = nc.dram_tensor("wg_b", [KC, 128, 2048], F32, kind="ExternalInput").ap()
    wo_b = nc.dram_tensor("wo_b", [KC, 128, 2048], F32, kind="ExternalInput").ap()
    gw_b = nc.dram_tensor("gw_b", [ICN, 128, 2048], F32, kind="ExternalInput").ap()
    uw_b = nc.dram_tensor("uw_b", [ICN, 128, 2048], F32, kind="ExternalInput").ap()
    dw_b = nc.dram_tensor("dw_b", [KC, IH, 128, SLOT], F32, kind="ExternalInput").ap()
    s_i_h = nc.dram_tensor("s_i", [128, KC], F32, kind="ExternalInput").ap()
    s_f_h = nc.dram_tensor("s_f", [128, KC], F32, kind="ExternalInput").ap()
    s_g_h = nc.dram_tensor("s_g", [128, KC], F32, kind="ExternalInput").ap()
    s_o_h = nc.dram_tensor("s_o", [128, KC], F32, kind="ExternalInput").ap()
    gnw_h = nc.dram_tensor("gnw", [128, KC], F32, kind="ExternalInput").ap()
    ds_h = nc.dram_tensor("ds", [128, KC], F32, kind="ExternalInput").ap()
    gs_h = nc.dram_tensor("gs", [128, ICN], F32, kind="ExternalInput").ap()
    us_h = nc.dram_tensor("us", [128, ICN], F32, kind="ExternalInput").ap()
    cmask_h = nc.dram_tensor("cmask", [128, 1], F32, kind="ExternalInput").ap()
    y = nc.dram_tensor("y", [128, KC * T], F32, kind="ExternalOutput").ap()

    x1_dram = nc.dram_tensor("x1_spill", [128, KC * T], F32)
    cc_in = nc.dram_tensor("cc_in", [128, KC], F32)
    cc_out = nc.dram_tensor("cc_out", [256, KC], F32)
    RG = [[0, 1], [2, 3], [4, 5], [6, 7]]

    from contextlib import ExitStack
    ctx = ExitStack()
    sb = lambda name, shape, dt: ctx.enter_context(nc.sbuf_tensor(name, shape, dt))
    ps = lambda name, shape: ctx.enter_context(nc.psum_tensor(name, shape, F32))

    x_s = sb("x_s", [128, KC * T], F32)
    xn_s = sb("xn_s", [128, KC * T], BF16)
    hd_s = sb("hd_s", [128, ICH * T], BF16)
    wbuf = sb("wbuf", [128, 4 * SLOT], BF16)
    invB = sb("invB", [128, T], BF16)
    minvB = sb("minvB", [128, T], BF16)
    sc1 = sb("sc1", [128, T], F32)
    sc2 = sb("sc2", [128, T], F32)
    sq = sb("sq", [128, T], F32)
    x1buf = sb("x1buf", [128, 2 * T], F32)
    lnx = sb("lnx", [128, 4 * T], BF16)
    r1 = sb("r1", [1, T], F32)
    r2 = sb("r2", [1, T], F32)
    ones_rb = sb("ones_rb", [128, 1], BF16)
    ones_b = sb("ones_b", [1, 128], F32)
    invD_b = sb("invD_b", [1, 128], BF16)
    sqD_b = sb("sqD_b", [1, 128], BF16)
    sqD256_b = sb("sqD256_b", [1, 128], BF16)
    epsD = sb("epsD", [128, 1], F32)
    eps128 = sb("eps128", [128, 1], F32)
    s_i = sb("s_i_s", [128, KC], F32)
    s_f = sb("s_f_s", [128, KC], F32)
    s_g = sb("s_g_s", [128, KC], F32)
    s_o = sb("s_o_s", [128, KC], F32)
    gnw = sb("gnw_s", [128, KC], F32)
    ds = sb("ds_s", [128, KC], F32)
    gnwso = sb("gnwso", [128, KC], F32)
    dsh = sb("dsh", [128, KC], F32)
    gate_s = sb("gate_s_s", [128, ICN], F32)
    up_s = sb("up_s_s", [128, ICN], F32)
    cmask = sb("cmask_s", [128, 1], F32)
    hend = sb("hend", [128, KC], F32)
    carry = sb("carry", [128, KC], F32)

    p = [ps(f"p{i}", [128, T]) for i in range(4)]

    xF = x_s[:]
    Pb = x_s[:].bitcast(BF16)
    hdF = hd_s[:].bitcast(F32)
    xnF = xn_s[:].bitcast(F32)
    iQ = [sq[:].bitcast(I32), sc2[:].bitcast(I32)]
    r1b = r1[:].bitcast(BF16)[:, 0:T]
    r2b = r1[:].bitcast(BF16)[:, T:2 * T]
    sqbuf = [sq, sc2]


    def xs(k):
        return xF[:, k * T:(k + 1) * T]

    def xsth(k, th):
        return xF[:, k * T + th * 512: k * T + (th + 1) * 512]

    def xnth(k, th):
        return xn_s[:, k * T + th * 512: k * T + (th + 1) * 512]

    def hdth(k, th):
        return hd_s[:, k * T + th * 512: k * T + (th + 1) * 512]

    def hv(ec):
        return hd_s[:, ec * T:(ec + 1) * T]

    def Pv(ec):
        return Pb[:, ec * T:(ec + 1) * T]

    def slot(i, width=SLOT):
        return wbuf[:, i * SLOT: i * SLOT + width]

    f_sb = [xF[:, 8192 + j * 2048: 8192 + j * 2048 + T] for j in range(2)]
    i_sb = [xF[:, 8192 + j * 2048 + T: 8192 + (j + 1) * 2048] for j in range(2)]

    B = Build(nc)
    pcons = {i: ("ev", 0) for i in range(4)}
    wcons = {i: ("mm", 0) for i in range(4)}

    def use_psum(i, newcons=None):
        sem, val = pcons[i]
        B.wait("t", sem, val)

    def set_pcons(i, sem, val):
        pcons[i] = (sem, val)

    B.op("g", lambda e, S: e.memset(ones_rb[:], 1.0), "wi", 1)
    B.op("g", lambda e, S: e.memset(ones_b[:], 1.0), "wi", 1)
    B.op("g", lambda e, S: e.memset(eps128[:], 1e-5), "wi", 1)
    B.op("g", lambda e, S: e.memset(invD_b[:], 1.0 / D), "wi", 1)
    B.op("g", lambda e, S: e.memset(sqD_b[:], float(D) ** 0.5), "wi", 1)
    B.op("g", lambda e, S: e.memset(sqD256_b[:], 256.0 * float(D) ** 0.5), "wi", 1)
    B.op("g", lambda e, S: e.memset(epsD[:], 1e-5 * D), "wi", 1)
    for dst, src in [(s_i, s_i_h), (s_f, s_f_h), (s_g, s_g_h), (s_o, s_o_h),
                     (gnw, gnw_h), (ds, ds_h), (gate_s, gs_h), (up_s, us_h),
                     (cmask, cmask_h)]:
        B.op("g", lambda e, S, dst=dst, src=src: e.dma_start(out=dst[:], in_=src),
             "wi", 16)
    wd_init = B.cnt["wi"]
    B.wait("v", "wi", wd_init)
    for t_ in (s_i, s_f, s_g):
        B.op("v", lambda e, S, t_=t_: e.tensor_scalar_mul(t_[:], t_[:], 1.0 / 256.0),
             "ev")
    B.op("v", lambda e, S: e.tensor_mul(gnwso[:], gnw[:], s_o[:]), "ev")
    ev_init = B.op("v", lambda e, S: e.tensor_scalar_mul(dsh[:], ds[:], 0.5), "ev")

    _xsems = ["xa", "xb", "xc", "xd"]
    x_ready = []
    for grp in range(4):
        v = B.op("y", lambda e, S, grp=grp: e.dma_start(
            out=xF[:, grp * 4 * T:(grp + 1) * 4 * T],
            in_=x0[:, grp * 4 * T:(grp + 1) * 4 * T]), _xsems[grp], 16)
        x_ready += [(_xsems[grp], v)] * 4

    def ln_state():
        return {"sc_sq": [0] * KC, "mm_sq": [0] * KC, "ev_cst": [0] * KC}

    def emit_ln_chunk_sv(st, k, x_rdy, xn_clear=0):
        base = (k % 2) * 2 * T
        B.wait("s", x_rdy[0], x_rdy[1])
        B.wait("v", x_rdy[0], x_rdy[1])
        if k == 0 and xn_clear:
            B.wait("v", "mm", xn_clear)
        if k >= 2:
            B.wait("s", "mm", st["mm_sq"][k - 2])
        st["sc_sq"][k] = B.op("s", lambda e, S, k=k, base=base: e.activation(
            lnx[:, base + T:base + 2 * T], xs(k), ACT.Square), "sc")
        st["ev_cst"][k] = B.op("v", lambda e, S, k=k: e.tensor_copy(
            xn_s[:, k * T:(k + 1) * T], xs(k)), "ev")

    def emit_ln_chunk_t(st, k, rp):
        base = (k % 2) * 2 * T
        if k == 0:
            use_psum(rp[0])
            use_psum(rp[1])
        B.wait("t", "ev", st["ev_cst"][k])
        B.wait("t", "sc", st["sc_sq"][k])
        for th in range(TH):
            B.op("t", lambda e, S, k=k, th=th, rp=rp: e.matmul(
                p[rp[0]][0:1, th * 512:(th + 1) * 512], ones_rb[:],
                xnth(k, th), start=(k == 0), stop=(k == KC - 1),
                tile_position=(0, 0)))
            last = (th == TH - 1)
            fn = lambda e, S, k=k, th=th, base=base, rp=rp: e.matmul(
                p[rp[1]][32:33, th * 512:(th + 1) * 512], ones_rb[:],
                lnx[:, base + T + th * 512: base + T + (th + 1) * 512],
                start=(k == 0), stop=(k == KC - 1), tile_position=(0, 32))
            if last:
                st["mm_sq"][k] = B.op("t", fn, "mm")
            else:
                B.op("t", fn)

    def emit_ln_chunk(st, k, x_rdy, rp):
        emit_ln_chunk_sv(st, k, x_rdy)
        emit_ln_chunk_t(st, k, rp)

    def emit_ln_tail(st, fix, xn_wclear_mm, rp, bp):
        mm_red = st["mm_sq"][KC - 1]
        B.wait("v", "mm", mm_red)
        B.wait("s", "mm", mm_red)
        sc_m = B.op("s", lambda e, S, rp=rp: e.copy(r1b, p[rp[0]][0:1, :]), "sc")
        B.wait("v", "sc", sc_m)
        B.op("v", lambda e, S: e.tensor_mul(r2[:], r1b, r1b), "ev")
        ev_var = B.op("v", lambda e, S, rp=rp: e.scalar_tensor_tensor(
            r2[:], r2[:], -1.0 / D, p[rp[1]][32:33, :], ALU.mult, ALU.add), "ev")
        set_pcons(rp[0], "ev", ev_var)
        set_pcons(rp[1], "ev", ev_var)
        B.wait("s", "ev", ev_var)
        B.op("s", lambda e, S: e.activation(
            r2[:], r2[:], ACT.Ln, bias=epsD[0:1, :], scale=1.0), "sc")
        sc_exp = B.op("s", lambda e, S: e.activation(
            r2b, r2[:], ACT.Exp, bias=0.0, scale=-0.5), "sc")
        ev_rows = None
        for kb_sem, kb_val in (("sc", sc_m), ("ev", ev_var), ("sc", sc_exp)):
            B.wait("t", kb_sem, kb_val)
            B.op("t", lambda e, S, rp=rp: e.matmul(
                p[rp[0]][64:65, 0:64], ones_rb[:], lnx[:, 0:64],
                start=True, stop=True, tile_position=(0, 64)))
        B.wait("t", "sc", sc_exp)
        use_psum(bp[0])
        use_psum(bp[1])
        invrow = sqD256_b if fix else sqD_b
        for th in range(TH):
            B.op("t", lambda e, S, th=th, bp=bp, invrow=invrow: e.matmul(
                p[bp[0]][:, th * 512:(th + 1) * 512], invrow[:],
                r2b[0:1, th * 512:(th + 1) * 512], start=True, stop=True))
        mm_bc = None
        for th in range(TH):
            mm_bc = B.op("t", lambda e, S, th=th, bp=bp: e.matmul(
                p[bp[1]][:, th * 512:(th + 1) * 512], invD_b[:],
                r1b[0:1, th * 512:(th + 1) * 512], start=True, stop=True), "mm")
        B.wait("s", "mm", mm_bc)
        B.op("s", lambda e, S, bp=bp: e.copy(invB[:], p[bp[0]][:]), "sc")
        sc_bc = B.op("s", lambda e, S, bp=bp: e.copy(minvB[:], p[bp[1]][:]), "sc")
        set_pcons(bp[0], "sc", sc_bc)
        set_pcons(bp[1], "sc", sc_bc)
        B.wait("v", "sc", sc_bc)
        ev_ap = sc_cp = 0
        rdy = []
        sc_cp_k = [0] * KC
        for k in range(KC):
            xk = xn_s[:, k * T:(k + 1) * T]
            B.op("v", lambda e, S, xk=xk: e.tensor_sub(xk, xk, minvB[:]), "ev")
            if not fix:
                ev_ap = B.op("v", lambda e, S, xk=xk: e.tensor_mul(
                    xk, xk, invB[:]), "ev")
                rdy.append(("ev", ev_ap))
            else:
                if k >= 2:
                    B.wait("v", "sc", sc_cp_k[k - 2])
                ev_ap = B.op("v", lambda e, S, k=k, xk=xk: e.tensor_mul(
                    iQ[k % 2][:, 0:T], xk, invB[:]), "ev")
                B.wait("s", "ev", ev_ap)
                val = B.op("s", lambda e, S, k=k: e.copy(
                    xn_s[:, k * T:(k + 1) * T], iQ[k % 2][:, 0:T]), "sc")
                sc_cp_k[k] = val
                sc_cp = val
                rdy.append(("sc", val))
        xn_last = ("sc", sc_cp) if fix else rdy[-1]
        return xn_last, rdy

    def emit_ffn_prefetch():
        vals = {}
        for s_id, srcw, blk in ((0, gw_b, 0), (1, uw_b, 0), (2, gw_b, 1),
                                (3, uw_b, 1)):
            sem, val = wcons[s_id]
            B.wait("g", sem, val)
            vals[s_id] = B.op("g", lambda e, S, s_id=s_id, srcw=srcw, blk=blk:
                              e.dma_start(out=slot(s_id, 2048), in_=srcw[blk]),
                              "w%d" % s_id, 16)
        return vals

    def emit_ffn(xn_ready, post_cb=None, pre=None):
        xn_last, xn_rdy = xn_ready
        resid_ev = 0
        resid_list = [0] * KC
        for ih in range(IH):
            sc_silu = {}
            hdn_ev = {}
            mm_gate = {}
            mm_up = {}
            for idx in range(ICH):
                ic = ih * ICH + idx
                par = idx % 2
                sg, su = 2 * par, 2 * par + 1
                if pre is not None and ih == 0 and idx < 2:
                    wd_g, wd_u = pre[sg], pre[su]
                else:
                    for s_id, src in ((sg, gw_b), (su, uw_b)):
                        sem, val = wcons[s_id]
                        B.wait("g", sem, val)
                        wdv = B.op("g", lambda e, S, s_id=s_id, src=src, ic=ic:
                                   e.dma_start(out=slot(s_id, 2048), in_=src[ic]),
                                   "w%d" % s_id, 16)
                        if s_id == sg:
                            wd_g = wdv
                        else:
                            wd_u = wdv
                B.wait("t", "w%d" % sg, wd_g)
                ladder = (idx == 0 and ih == 0)
                pg = p[0 if par == 0 else 2]
                pu = p[1 if par == 0 else 3]
                sem, val = pcons[0 if par == 0 else 2]
                B.wait("t", sem, val)
                for kc in range(KC):
                    if ladder:
                        B.wait("t", xn_rdy[kc][0], xn_rdy[kc][1])
                    for th in range(TH):
                        last = (th == TH - 1 and kc == KC - 1)
                        fn = lambda e, S, th=th, kc=kc, sg=sg, pg=pg: e.matmul(
                            pg[:, th * 512:(th + 1) * 512],
                            slot(sg, 2048)[:, kc * 128:(kc + 1) * 128],
                            xnth(kc, th), start=(kc == 0), stop=(kc == KC - 1))
                        if last:
                            mm_gate[idx] = B.op("t", fn, "mm")
                        else:
                            B.op("t", fn)
                wcons[sg] = ("mm", mm_gate[idx])
                B.wait("t", "w%d" % su, wd_u)
                sem, val = pcons[1 if par == 0 else 3]
                B.wait("t", sem, val)
                for kc in range(KC):
                    for th in range(TH):
                        last = (th == TH - 1 and kc == KC - 1)
                        fn = lambda e, S, th=th, kc=kc, su=su, pu=pu: e.matmul(
                            pu[:, th * 512:(th + 1) * 512],
                            slot(su, 2048)[:, kc * 128:(kc + 1) * 128],
                            xnth(kc, th), start=(kc == 0), stop=(kc == KC - 1))
                        if last:
                            mm_up[idx] = B.op("t", fn, "mm")
                        else:
                            B.op("t", fn)
                wcons[su] = ("mm", mm_up[idx])
                B.wait("s", "mm", mm_gate[idx])
                if idx >= 2:
                    B.wait("s", "ev", hdn_ev[idx - 2])
                if ih == 0 and idx < 2:
                    B.wait("s", "ev", ev_init)
                sc_silu[idx] = B.op("s", lambda e, S, par=par, ic=ic, pg=pg:
                                    e.activation(sqbuf[par][:], pg[:], ACT.Silu,
                                                 scale=gate_s[:, ic:ic + 1]), "sc")
                pcons[0 if par == 0 else 2] = ("sc", sc_silu[idx])
                B.wait("v", "mm", mm_up[idx])
                B.wait("v", "sc", sc_silu[idx])
                hdn_ev[idx] = B.op("v", lambda e, S, idx=idx, ic=ic, par=par, pu=pu:
                                   e.scalar_tensor_tensor(
                                       hd_s[:, idx * T:(idx + 1) * T], pu[:],
                                       up_s[:, ic:ic + 1], sqbuf[par][:],
                                       ALU.mult, ALU.mult), "ev")
                pcons[1 if par == 0 else 3] = ("ev", hdn_ev[idx])
            for oc in range(KC):
                par = oc % 2
                sd = 0 if par == 0 else 2
                sem, val = wcons[sd]
                B.wait("g", sem, val)
                wd_d = B.op("g", lambda e, S, sd=sd, oc=oc, ih=ih: e.dma_start(
                    out=slot(sd), in_=dw_b[oc, ih]), "w%d" % sd, 16)
                B.wait("t", "w%d" % sd, wd_d)
                if oc == 0:
                    B.wait("t", "ev", hdn_ev[ICH - 1])
                pd = p[0 if par == 0 else 2]
                sem, val = pcons[0 if par == 0 else 2]
                B.wait("t", sem, val)
                for kc in range(ICH):
                    for th in range(TH):
                        last = (th == TH - 1 and kc == ICH - 1)
                        fn = lambda e, S, th=th, kc=kc, sd=sd, pd=pd: e.matmul(
                            pd[:, th * 512:(th + 1) * 512],
                            slot(sd)[:, kc * 128:(kc + 1) * 128],
                            hdth(kc, th), start=(kc == 0), stop=(kc == ICH - 1))
                        if last:
                            mm_d = B.op("t", fn, "mm")
                        else:
                            B.op("t", fn)
                wcons[sd] = ("mm", mm_d)
                B.wait("v", "mm", mm_d)
                resid_ev = B.op("v", lambda e, S, oc=oc, pd=pd:
                                e.scalar_tensor_tensor(
                                    xs(oc), pd[:], dsh[:, oc:oc + 1], xs(oc),
                                    ALU.mult, ALU.add), "ev")
                pcons[0 if par == 0 else 2] = ("ev", resid_ev)
                resid_list[oc] = resid_ev
                if post_cb is not None and ih == IH - 1:
                    post_cb(oc, resid_ev, xn_clear=mm_up[ICH - 1])
        return resid_ev, resid_list

    def lagged_ln_cb(st, rp):
        pend = []

        def cb(oc, ev, xn_clear=0):
            if len(pend) >= 2:
                emit_ln_chunk_t(st, pend.pop(0), rp)
            emit_ln_chunk_sv(st, oc, ("ev", ev), xn_clear)
            pend.append(oc)

        def flush():
            while pend:
                emit_ln_chunk_t(st, pend.pop(0), rp)
        return cb, flush

    ln1 = ln_state()
    for k in range(KC):
        if k >= 1:
            emit_ln_chunk_t(ln1, k - 1, (1, 3))
        emit_ln_chunk_sv(ln1, k, x_ready[k])
    emit_ln_chunk_t(ln1, KC - 1, (1, 3))
    ffn1_pre = emit_ffn_prefetch()
    xn_last1, xn_rdy1 = emit_ln_tail(ln1, fix=False, xn_wclear_mm=0,
                                     rp=(1, 3), bp=(0, 2))
    ln2 = ln_state()
    ln2_cb, ln2_flush = lagged_ln_cb(ln2, (1, 3))
    x1_ev, x1_list = emit_ffn((xn_last1, xn_rdy1), post_cb=ln2_cb, pre=ffn1_pre)
    ln2_flush()
    last_xn_mm = B.cnt["mm"]
    attn_pre = {}
    for s_id, srcw, blk in ((0, wf_b, 0), (1, wi_b, 0), (2, wf_b, 1), (3, wi_b, 1)):
        sem, val = wcons[s_id]
        B.wait("g", sem, val)
        attn_pre[s_id] = B.op("g", lambda e, S, s_id=s_id, srcw=srcw, blk=blk:
                              e.dma_start(out=slot(s_id, 2048), in_=srcw[blk]),
                              "w%d" % s_id, 16)
    xn2_last, xn2_rdy = emit_ln_tail(ln2, fix=True, xn_wclear_mm=last_xn_mm,
                                     rp=(1, 3), bp=(0, 2))

    B.wait("y", "ev", B.cnt["ev"])
    xd_spill = B.op("y", lambda e, S: e.dma_start(out=x1_dram.ap(), in_=xF), "xs", 16)

    wg_pre_buf = [hd_s[:, (16 + 2 * j) * T:(16 + 2 * j + 2) * T] for j in range(3)]
    wg_pre_buf += [sq[:].bitcast(BF16), sc2[:].bitcast(BF16), sc1[:].bitcast(BF16)]
    _wg_sems = ["xe", "xf", "xg", "xh", "gv", "wi"]
    wg_pre_wd = [0] * 6
    for j in range(5):
        B.wait("g", "ev", x1_ev)
        if j >= 3:
            B.wait("g", xn2_last[0], xn2_last[1])
        wg_pre_wd[j] = B.op("g", lambda e, S, j=j: e.dma_start(
            out=wg_pre_buf[j], in_=wg_b[j]), _wg_sems[j], 16)
    scan_ev = [0] * KC
    sc_f = [0] * KC
    mm_f = [0] * KC
    mm_i = [0] * KC
    for ec in range(KC):
        par = ec % 2
        sfslot, sislot = 2 * par, 2 * par + 1
        if ec < 2:
            wd_f, wd_i = attn_pre[sfslot], attn_pre[sislot]
        else:
            for s_id, src in ((sfslot, wf_b), (sislot, wi_b)):
                sem, val = wcons[s_id]
                B.wait("g", sem, val)
                wdv = B.op("g", lambda e, S, s_id=s_id, src=src, ec=ec: e.dma_start(
                    out=slot(s_id, 2048), in_=src[ec]), "w%d" % s_id, 16)
                if s_id == sfslot:
                    wd_f = wdv
                else:
                    wd_i = wdv
        B.wait("t", "w%d" % sfslot, wd_f)
        pf = p[0 if par == 0 else 2]
        pi = p[1 if par == 0 else 3]
        sem, val = pcons[0 if par == 0 else 2]
        B.wait("t", sem, val)
        for kc in range(KC):
            if ec == 0:
                B.wait("t", xn2_rdy[kc][0], xn2_rdy[kc][1])
            for th in range(TH):
                last = (th == TH - 1 and kc == KC - 1)
                fn = lambda e, S, th=th, kc=kc, sfslot=sfslot, pf=pf: e.matmul(
                    pf[:, th * 512:(th + 1) * 512],
                    slot(sfslot, 2048)[:, kc * 128:(kc + 1) * 128],
                    xnth(kc, th), start=(kc == 0), stop=(kc == KC - 1))
                if last:
                    mm_f[ec] = B.op("t", fn, "mm")
                else:
                    B.op("t", fn)
        wcons[sfslot] = ("mm", mm_f[ec])
        B.wait("t", "w%d" % sislot, wd_i)
        sem, val = pcons[1 if par == 0 else 3]
        B.wait("t", sem, val)
        for kc in range(KC):
            for th in range(TH):
                last = (th == TH - 1 and kc == KC - 1)
                fn = lambda e, S, th=th, kc=kc, sislot=sislot, pi=pi: e.matmul(
                    pi[:, th * 512:(th + 1) * 512],
                    slot(sislot, 2048)[:, kc * 128:(kc + 1) * 128],
                    xnth(kc, th), start=(kc == 0), stop=(kc == KC - 1))
                if last:
                    mm_i[ec] = B.op("t", fn, "mm")
                else:
                    B.op("t", fn)
        wcons[sislot] = ("mm", mm_i[ec])
        B.wait("s", "mm", mm_f[ec])
        if ec == 0:
            B.wait("s", "xs", xd_spill)
        if ec >= 2:
            B.wait("s", "ev", scan_ev[ec - 2])
        sc_f[ec] = B.op("s", lambda e, S, ec=ec, par=par, pf=pf: e.activation(
            f_sb[par], pf[:], ACT.Sigmoid, scale=s_f[:, ec:ec + 1]), "sc")
        pcons[0 if par == 0 else 2] = ("sc", sc_f[ec])
        B.wait("v", "mm", mm_i[ec])
        B.wait("v", "sc", sc_f[ec])
        if ec == 0:
            B.wait("v", "xs", xd_spill)
        B.op("v", lambda e, S, ec=ec, par=par, pi=pi: e.scalar_tensor_tensor(
            sc1[:], pi[:], s_i[:, ec:ec + 1], f_sb[par], ALU.mult, ALU.mult), "ev")
        ev_i = B.op("v", lambda e, S, ec=ec, par=par, pi=pi: e.scalar_tensor_tensor(
            i_sb[par], pi[:], s_i[:, ec:ec + 1], sc1[:], ALU.mult, ALU.subtract),
            "ev")
        pcons[1 if par == 0 else 3] = ("ev", ev_i)
        B.op("v", lambda e, S, ec=ec, par=par: e.tensor_tensor_scan(
            hv(ec), f_sb[par], i_sb[par], 0.0, ALU.mult, ALU.add), "ev")
        B.op("v", lambda e, S, ec=ec, par=par: e.tensor_tensor_scan(
            Pv(ec), f_sb[par], f_sb[par], 1.0, ALU.mult, ALU.bypass), "ev")
        scan_ev[ec] = B.op("v", lambda e, S, ec=ec: e.tensor_copy(
            hend[:, ec:ec + 1], hv(ec)[:, T - 1:T]), "ev")

    wd_g2 = [0] * KC
    B.wait("g", "ev", scan_ev[KC - 1])
    wg_pre_wd[5] = B.op("g", lambda e, S: e.dma_start(
        out=wg_pre_buf[5], in_=wg_b[5]), _wg_sems[5], 16)
    wd_ci = B.cnt["wi"] + 16
    wd_carry = B.cnt["wi"] + 32
    _cw = {0: B.cnt["w0"], 2: B.cnt["w2"]}
    for ec in range(6, KC):
        sgslot = 2 * (ec % 2)
        _cw[sgslot] += 16
        wd_g2[ec] = _cw[sgslot]
    B.wait("v", "wi", wd_carry)
    B.op("v", lambda e, S: e.tensor_scalar_mul(carry[:], carry[:], cmask[:, 0:1]),
         "ev")
    fix_ev = [0] * KC
    for ec in range(KC):
        fix_ev[ec] = B.op("v", lambda e, S, ec=ec: e.scalar_tensor_tensor(
            hv(ec), Pv(ec), carry[:, ec:ec + 1], hv(ec), ALU.mult, ALU.add), "ev")

    def sg_store(ec):
        return Pb[:, 16384 + ec * T: 16384 + (ec + 1) * T]

    sq2_ev = [0] * KC
    mm_sq2 = [0] * KC
    sc_silu2 = [0] * KC
    u2_ev = [0] * KC
    x1b_bf = x1buf[:].bitcast(BF16)
    sqst = [lnx[:, j * T:(j + 1) * T] for j in range(4)] + \
           [x1b_bf[:, j * T:(j + 1) * T] for j in range(4)]

    def emit_rms_red(ec):
        if ec == 0:
            sem, val = pcons[3]
            B.wait("t", sem, val)
        B.wait("t", "ev", sq2_ev[ec])
        for th in range(TH):
            last = (th == TH - 1)
            fn = lambda e, S, ec=ec, th=th: e.matmul(
                p[3][0:1, th * 512:(th + 1) * 512], ones_rb[:],
                sqst[ec % 8][:, th * 512:(th + 1) * 512],
                start=(ec == 0), stop=(ec == KC - 1))
            if last:
                mm_sq2[ec] = B.op("t", fn, "mm")
            else:
                B.op("t", fn)

    def emit_sq_u2(k):
        if k >= 8:
            B.wait("v", "mm", mm_sq2[k - 8])
        sq2_ev[k] = B.op("v", lambda e, S, k=k: e.tensor_mul(
            sqst[k % 8], hv(k), hv(k)), "ev")
        B.wait("v", "sc", sc_silu2[k])
        u2_ev[k] = B.op("v", lambda e, S, k=k: e.scalar_tensor_tensor(
            hv(k), hv(k), gnwso[:, k:k + 1], sg_store(k), ALU.mult, ALU.mult), "ev")

    mm_g_list = [0] * KC
    for ec in range(KC):
        par = ec % 2
        sgslot = 2 * par
        if ec < 6:
            B.wait("t", _wg_sems[ec], wg_pre_wd[ec])
            wsrc = wg_pre_buf[ec]
        else:
            B.wait("t", "w%d" % sgslot, wd_g2[ec])
            wsrc = slot(sgslot, 2048)
        pg = p[0 if par == 0 else 2]
        sem, val = pcons[0 if par == 0 else 2]
        B.wait("t", sem, val)
        for kc in range(KC):
            for th in range(TH):
                last = (th == TH - 1 and kc == KC - 1)
                fn = lambda e, S, th=th, kc=kc, wsrc=wsrc, pg=pg: e.matmul(
                    pg[:, th * 512:(th + 1) * 512],
                    wsrc[:, kc * 128:(kc + 1) * 128],
                    xnth(kc, th), start=(kc == 0), stop=(kc == KC - 1))
                if last:
                    mm_g = B.op("t", fn, "mm")
                else:
                    B.op("t", fn)
        mm_g_list[ec] = mm_g
        if ec >= 8:
            emit_rms_red(ec - 8)
        B.wait("s", "mm", mm_g)
        if ec == 0:
            B.wait("s", "ev", scan_ev[KC - 1])
        sc_silu2[ec] = B.op("s", lambda e, S, ec=ec, pg=pg: e.activation(
            sg_store(ec), pg[:], ACT.Silu, scale=s_g[:, ec:ec + 1]), "sc")
        pcons[0 if par == 0 else 2] = ("sc", sc_silu2[ec])
        if ec >= 2:
            emit_sq_u2(ec - 2)
    for k in range(KC - 2, KC):
        emit_sq_u2(k)
    for ec in range(KC - 8, KC):
        emit_rms_red(ec)
    v = B.op("g", lambda e, S: e.dma_start(out=cc_in.ap(), in_=hend[:]), "wi", 16)
    assert v == wd_ci
    B.wait("g", "wi", wd_ci)
    cc_v = B.op("g", lambda e, S: e.collective_compute(
        "AllGather", ALU.bypass, replica_groups=RG,
        ins=[cc_in.ap().opt()], outs=[cc_out.ap().opt()]), "cc", 1)

    def emit_wg_dma(ec):
        sgslot = 2 * (ec % 2)
        if ec in (6, 7):
            sem, val = wcons[sgslot]
        else:
            sem, val = "mm", mm_g_list[ec - 2]
        B.wait("g", sem, val)
        v = B.op("g", lambda e, S, sgslot=sgslot, ec=ec: e.dma_start(
            out=slot(sgslot, 2048), in_=wg_b[ec]), "w%d" % sgslot, 16)
        assert v == wd_g2[ec], (ec, v, wd_g2[ec])

    emit_wg_dma(6)
    emit_wg_dma(7)
    B.wait("g", "cc", cc_v)
    v = B.op("g", lambda e, S: e.dma_start(
        out=carry[:], in_=cc_out.ap()[0:128, :]), "wi", 16)
    assert v == wd_carry
    for ec in range(8, KC):
        emit_wg_dma(ec)
    wcons[0] = ("mm", mm_g_list[KC - 2])
    wcons[2] = ("mm", mm_g_list[KC - 1])
    mm_g_last = B.cnt["mm"]
    mm_rms = mm_sq2[KC - 1]
    set_pcons(3, "mm", mm_rms)
    B.wait("s", "mm", mm_rms)
    sc_rs = B.op("s", lambda e, S: e.activation(
        r2[:], p[3][0:1, :], ACT.Ln, bias=epsD[0:1, :], scale=1.0), "sc")
    set_pcons(3, "sc", sc_rs)
    sc_re = B.op("s", lambda e, S: e.activation(
        r2b, r2[:], ACT.Exp, bias=0.0, scale=-0.5), "sc")
    for kb_sem, kb_val in (("sc", sc_rs), ("sc", sc_re)):
        B.wait("t", kb_sem, kb_val)
        B.op("t", lambda e, S: e.matmul(
            p[3][64:65, 0:64], ones_rb[:], lnx[:, 0:64],
            start=True, stop=True, tile_position=(0, 64)))
    B.wait("t", "sc", sc_re)
    sem, val = pcons[1]
    B.wait("t", sem, val)
    mm_rb = None
    for th in range(TH):
        mm_rb = B.op("t", lambda e, S, th=th: e.matmul(
            p[1][:, th * 512:(th + 1) * 512], sqD_b[:],
            r2b[0:1, th * 512:(th + 1) * 512], start=True, stop=True), "mm")
    B.wait("s", "mm", mm_rb)
    sc_rb = B.op("s", lambda e, S: e.copy(invB[:], p[1][:]), "sc")
    set_pcons(1, "sc", sc_rb)
    o_ev = [0] * KC
    B.wait("v", "sc", sc_rb)
    for ec in range(KC):
        o_ev[ec] = B.op("v", lambda e, S, ec=ec: e.tensor_mul(
            hv(ec), hv(ec), invB[:]), "ev")

    ln3 = ln_state()
    ln3_cb, ln3_flush = lagged_ln_cb(ln3, (0, 2))
    x1cons = [0, 0]
    x2_ev = [0] * KC
    for oc in range(KC):
        par = oc % 2
        soslot = 2 * par + 1
        sem, val = wcons[soslot]
        B.wait("g", sem, val)
        wd_o = B.op("g", lambda e, S, soslot=soslot, oc=oc: e.dma_start(
            out=slot(soslot, 2048), in_=wo_b[oc]), "w%d" % soslot, 16)
        if oc < 2:
            B.wait("y", "mm", mm_sq2[KC - 1])
        B.wait("y", "ev", x1cons[par])
        xsem = "xa" if par == 0 else "xb"
        xd_x1 = B.op("y", lambda e, S, oc=oc, par=par: e.dma_start(
            out=x1buf[:, par * T:(par + 1) * T],
            in_=x1_dram.ap()[:, oc * T:(oc + 1) * T]), xsem, 16)
        B.wait("t", "w%d" % soslot, wd_o)
        po = p[1 if par == 0 else 3]
        sem, val = pcons[1 if par == 0 else 3]
        B.wait("t", sem, val)
        for kc in range(KC):
            if oc == 0:
                B.wait("t", "ev", o_ev[kc])
            for th in range(TH):
                last = (th == TH - 1 and kc == KC - 1)
                fn = lambda e, S, th=th, kc=kc, soslot=soslot, po=po: e.matmul(
                    po[:, th * 512:(th + 1) * 512],
                    slot(soslot, 2048)[:, kc * 128:(kc + 1) * 128],
                    hdth(kc, th), start=(kc == 0), stop=(kc == KC - 1))
                if last:
                    mm_o = B.op("t", fn, "mm")
                else:
                    B.op("t", fn)
        wcons[soslot] = ("mm", mm_o)
        B.wait("v", "mm", mm_o)
        B.wait("v", xsem, xd_x1)
        B.op("v", lambda e, S, oc=oc, par=par, po=po: e.tensor_scalar_mul(
            iQ[par][:, 0:T], po[:], 256.0), "ev")
        x2_ev[oc] = B.op("v", lambda e, S, oc=oc, par=par: e.scalar_tensor_tensor(
            xs(oc), iQ[par][:, 0:T], 1.0 / 256.0, x1buf[:, par * T:(par + 1) * T],
            ALU.mult, ALU.add), "ev")
        pcons[1 if par == 0 else 3] = ("ev", x2_ev[oc])
        x1cons[par] = x2_ev[oc]
        ln3_cb(oc, x2_ev[oc], xn_clear=mm_g_last)

    last_xn_mm2 = B.cnt["mm"]
    ln3_flush()
    ffn2_pre = emit_ffn_prefetch()
    xn3_last, xn3_rdy = emit_ln_tail(ln3, fix=False, xn_wclear_mm=last_xn_mm2,
                                     rp=(0, 2), bp=(1, 3))
    x3_ev, x3_list = emit_ffn((xn3_last, xn3_rdy), pre=ffn2_pre)

    xd_out = 0
    for oc in range(KC):
        B.wait("y", "ev", x3_list[oc])
        xd_out = B.op("y", lambda e, S, oc=oc: e.dma_start(
            out=y[:, oc * T:(oc + 1) * T], in_=xs(oc)), "xs", 16)
    B.wait("y", "xs", xd_out)

    from contextlib import ExitStack as _ES
    sem_ctx = _ES()
    S = {}
    for nm in ["mm", "sc", "ev", "cc", "gv", "wi", "w0", "w1", "w2", "w3",
               "xa", "xb", "xc", "xd", "xe", "xf", "xg", "xh", "xs"]:
        S[nm] = sem_ctx.enter_context(nc.semaphore("s_" + nm))
    with nc.Block() as block:

        @block.gpsimd
        def _(e):
            for fn in B.q["g"]:
                fn(e, S)

        @block.sync
        def _(e):
            for fn in B.q["y"]:
                fn(e, S)

        @block.tensor
        def _(e):
            for fn in B.q["t"]:
                fn(e, S)

        @block.scalar
        def _(e):
            for fn in B.q["s"]:
                fn(e, S)

        @block.vector
        def _(e):
            for fn in B.q["v"]:
                fn(e, S)

    sem_ctx.close()
    ctx.close()
    return nc


def _chunked(v):
    n = v.shape[0] // 128
    return np.ascontiguousarray(v.reshape(n, 128).T)


def kernel(L0_attn_hidden_states, L0_attn_w_i, L0_attn_w_f, L0_attn_w_g,
           L0_attn_w_o, L0_attn_s_i, L0_attn_s_f, L0_attn_s_g, L0_attn_s_o,
           L0_attn_g_norm_w, L0_ffn_gate_w, L0_ffn_up_w, L0_ffn_down_w,
           L0_ffn_gate_s, L0_ffn_up_s, L0_ffn_down_s):
    f32 = np.float32
    x = np.asarray(L0_attn_hidden_states, f32)

    def blk_dxd(w):
        return np.ascontiguousarray(
            np.asarray(w, f32).reshape(KC, 128, KC, 128).transpose(2, 1, 0, 3)
            .reshape(KC, 128, 2048))

    wi_b = blk_dxd(L0_attn_w_i)
    wf_b = blk_dxd(L0_attn_w_f)
    wg_b = blk_dxd(L0_attn_w_g)
    wo_b = blk_dxd(np.asarray(L0_attn_w_o, f32).T)
    gw_b = np.ascontiguousarray(
        np.asarray(L0_ffn_gate_w, f32).T.reshape(KC, 128, ICN, 128)
        .transpose(2, 1, 0, 3).reshape(ICN, 128, 2048))
    uw_b = np.ascontiguousarray(
        np.asarray(L0_ffn_up_w, f32).T.reshape(KC, 128, ICN, 128)
        .transpose(2, 1, 0, 3).reshape(ICN, 128, 2048))
    dw_b = np.ascontiguousarray(
        np.asarray(L0_ffn_down_w, f32).T.reshape(IH, ICH, 128, KC, 128)
        .transpose(3, 0, 2, 1, 4).reshape(KC, IH, 128, SLOT))

    base = {
        "wi_b": wi_b, "wf_b": wf_b, "wg_b": wg_b, "wo_b": wo_b,
        "gw_b": gw_b, "uw_b": uw_b, "dw_b": dw_b,
        "s_i": _chunked(np.asarray(L0_attn_s_i, f32)),
        "s_f": _chunked(np.asarray(L0_attn_s_f, f32)),
        "s_g": _chunked(np.asarray(L0_attn_s_g, f32)),
        "s_o": _chunked(np.asarray(L0_attn_s_o, f32)),
        "gnw": _chunked(np.asarray(L0_attn_g_norm_w, f32)),
        "ds": _chunked(np.asarray(L0_ffn_down_s, f32)),
        "gs": _chunked(np.asarray(L0_ffn_gate_s, f32)),
        "us": _chunked(np.asarray(L0_ffn_up_s, f32)),
    }
    in_maps = []
    for c in range(8):
        b, h = c // 2, c % 2
        m = dict(base)
        m["x0"] = np.ascontiguousarray(
            x[b, h * T:(h + 1) * T, :].T.reshape(KC, 128, T)
            .transpose(1, 0, 2)).reshape(128, KC * T)
        m["cmask"] = np.full((128, 1), float(h), f32)
        in_maps.append(m)

    nc = build_nc()
    res = run_bass_kernel_spmd(nc, in_maps, core_ids=list(range(8)))
    out = np.empty((4, 2048, D), f32)
    for c in range(8):
        b, h = c // 2, c % 2
        yfm = res.results[c]["y"].reshape(128, KC, T).transpose(1, 0, 2)
        out[b, h * T:(h + 1) * T, :] = yfm.reshape(D, T).T
    return out


# version 25
# speedup vs baseline: 1.1849x; 1.1849x over previous
import numpy as np
import concourse.bass as bass
import concourse.mybir as mybir
from concourse.bass_utils import run_bass_kernel_spmd

F32 = mybir.dt.float32
BF16 = mybir.dt.bfloat16
I32 = mybir.dt.int32
ALU = mybir.AluOpType
ACT = mybir.ActivationFunctionType

D = 2048
I = 5632
T = 1024
KC = D // 128
ICN = I // 128
IH = 2
ICH = ICN // IH
TH = 2
SLOT = 2816


class Build:

    def __init__(self, nc):
        self.nc = nc
        self.q = {"g": [], "y": [], "t": [], "s": [], "v": []}
        self.cnt = {"mm": 0, "sc": 0, "ev": 0, "cc": 0, "gv": 0, "wi": 0,
                    "w0": 0, "w1": 0, "w2": 0, "w3": 0,
                    "xa": 0, "xb": 0, "xc": 0, "xd": 0,
                    "xe": 0, "xf": 0, "xg": 0, "xh": 0, "xs": 0}

    def wait(self, eng, sem, val):
        if val > 0:
            self.q[eng].append(lambda e, S, sem=sem, val=val: e.wait_ge(S[sem], val))

    def op(self, eng, fn, sem=None, inc=1):
        if sem is None:
            self.q[eng].append(lambda e, S, fn=fn: fn(e, S))
            return None
        self.cnt[sem] += inc
        after = self.cnt[sem]
        self.q[eng].append(
            lambda e, S, fn=fn, sem=sem, inc=inc: fn(e, S).then_inc(S[sem], inc))
        return after


def build_nc():
    nc = bass.Bass("TRN2", target_bir_lowering=False)

    x0 = nc.dram_tensor("x0", [128, KC * T], F32, kind="ExternalInput").ap()
    wi_b = nc.dram_tensor("wi_b", [KC, 128, 2048], F32, kind="ExternalInput").ap()
    wf_b = nc.dram_tensor("wf_b", [KC, 128, 2048], F32, kind="ExternalInput").ap()
    wg_b = nc.dram_tensor("wg_b", [KC, 128, 2048], F32, kind="ExternalInput").ap()
    wo_b = nc.dram_tensor("wo_b", [KC, 128, 2048], F32, kind="ExternalInput").ap()
    gw_b = nc.dram_tensor("gw_b", [ICN, 128, 2048], F32, kind="ExternalInput").ap()
    uw_b = nc.dram_tensor("uw_b", [ICN, 128, 2048], F32, kind="ExternalInput").ap()
    dw_b = nc.dram_tensor("dw_b", [KC, IH, 128, SLOT], F32, kind="ExternalInput").ap()
    s_i_h = nc.dram_tensor("s_i", [128, KC], F32, kind="ExternalInput").ap()
    s_f_h = nc.dram_tensor("s_f", [128, KC], F32, kind="ExternalInput").ap()
    s_g_h = nc.dram_tensor("s_g", [128, KC], F32, kind="ExternalInput").ap()
    s_o_h = nc.dram_tensor("s_o", [128, KC], F32, kind="ExternalInput").ap()
    gnw_h = nc.dram_tensor("gnw", [128, KC], F32, kind="ExternalInput").ap()
    ds_h = nc.dram_tensor("ds", [128, KC], F32, kind="ExternalInput").ap()
    gs_h = nc.dram_tensor("gs", [128, ICN], F32, kind="ExternalInput").ap()
    us_h = nc.dram_tensor("us", [128, ICN], F32, kind="ExternalInput").ap()
    cmask_h = nc.dram_tensor("cmask", [128, 1], F32, kind="ExternalInput").ap()
    y = nc.dram_tensor("y", [128, KC * T], F32, kind="ExternalOutput").ap()

    x1_dram = nc.dram_tensor("x1_spill", [128, KC * T], F32)
    cc_in = nc.dram_tensor("cc_in", [128, KC], F32)
    cc_out = nc.dram_tensor("cc_out", [256, KC], F32)
    RG = [[0, 1], [2, 3], [4, 5], [6, 7]]

    from contextlib import ExitStack
    ctx = ExitStack()
    sb = lambda name, shape, dt: ctx.enter_context(nc.sbuf_tensor(name, shape, dt))
    ps = lambda name, shape: ctx.enter_context(nc.psum_tensor(name, shape, F32))

    x_s = sb("x_s", [128, KC * T], F32)
    xn_s = sb("xn_s", [128, KC * T], BF16)
    hd_s = sb("hd_s", [128, ICH * T], BF16)
    wbuf = sb("wbuf", [128, 4 * SLOT], BF16)
    invB = sb("invB", [128, T], BF16)
    minvB = sb("minvB", [128, T], BF16)
    sc1 = sb("sc1", [128, T], F32)
    sc2 = sb("sc2", [128, T], F32)
    sq = sb("sq", [128, T], F32)
    x1buf = sb("x1buf", [128, 2 * T], F32)
    lnx = sb("lnx", [128, 4 * T], BF16)
    r1 = sb("r1", [1, T], F32)
    r2 = sb("r2", [1, T], F32)
    ones_rb = sb("ones_rb", [128, 1], BF16)
    ones_b = sb("ones_b", [1, 128], F32)
    invD_b = sb("invD_b", [1, 128], BF16)
    sqD_b = sb("sqD_b", [1, 128], BF16)
    sqD256_b = sb("sqD256_b", [1, 128], BF16)
    epsD = sb("epsD", [128, 1], F32)
    eps128 = sb("eps128", [128, 1], F32)
    s_i = sb("s_i_s", [128, KC], F32)
    s_f = sb("s_f_s", [128, KC], F32)
    s_g = sb("s_g_s", [128, KC], F32)
    s_o = sb("s_o_s", [128, KC], F32)
    gnw = sb("gnw_s", [128, KC], F32)
    ds = sb("ds_s", [128, KC], F32)
    gnwso = sb("gnwso", [128, KC], F32)
    dsh = sb("dsh", [128, KC], F32)
    gate_s = sb("gate_s_s", [128, ICN], F32)
    up_s = sb("up_s_s", [128, ICN], F32)
    cmask = sb("cmask_s", [128, 1], F32)
    hend = sb("hend", [128, KC], F32)
    carry = sb("carry", [128, KC], F32)

    p = [ps(f"p{i}", [128, T]) for i in range(4)]

    xF = x_s[:]
    Pb = x_s[:].bitcast(BF16)
    hdF = hd_s[:].bitcast(F32)
    xnF = xn_s[:].bitcast(F32)
    iQ = [sq[:].bitcast(mybir.dt.int16)[:, 0:T],
          sc2[:].bitcast(mybir.dt.int16)[:, 0:T]]
    iQ32 = [sq[:].bitcast(I32), sc2[:].bitcast(I32)]
    r1b = r1[:].bitcast(BF16)[:, 0:T]
    r2b = r1[:].bitcast(BF16)[:, T:2 * T]
    sqbuf = [sq, sc2]


    def xs(k):
        return xF[:, k * T:(k + 1) * T]

    def xsth(k, th):
        return xF[:, k * T + th * 512: k * T + (th + 1) * 512]

    def xnth(k, th):
        return xn_s[:, k * T + th * 512: k * T + (th + 1) * 512]

    def hdth(k, th):
        return hd_s[:, k * T + th * 512: k * T + (th + 1) * 512]

    def hv(ec):
        return hd_s[:, ec * T:(ec + 1) * T]

    def Pv(ec):
        return Pb[:, ec * T:(ec + 1) * T]

    def slot(i, width=SLOT):
        return wbuf[:, i * SLOT: i * SLOT + width]

    f_sb = [xF[:, 8192 + j * 2048: 8192 + j * 2048 + T] for j in range(2)]
    i_sb = [xF[:, 8192 + j * 2048 + T: 8192 + (j + 1) * 2048] for j in range(2)]

    B = Build(nc)
    pcons = {i: ("ev", 0) for i in range(4)}
    wcons = {i: ("mm", 0) for i in range(4)}

    def use_psum(i, newcons=None):
        sem, val = pcons[i]
        B.wait("t", sem, val)

    def set_pcons(i, sem, val):
        pcons[i] = (sem, val)

    B.op("g", lambda e, S: e.memset(ones_rb[:], 1.0), "wi", 1)
    B.op("g", lambda e, S: e.memset(ones_b[:], 1.0), "wi", 1)
    B.op("g", lambda e, S: e.memset(eps128[:], 1e-5), "wi", 1)
    B.op("g", lambda e, S: e.memset(invD_b[:], 1.0 / D), "wi", 1)
    B.op("g", lambda e, S: e.memset(sqD_b[:], float(D) ** 0.5), "wi", 1)
    B.op("g", lambda e, S: e.memset(sqD256_b[:], 256.0 * float(D) ** 0.5), "wi", 1)
    B.op("g", lambda e, S: e.memset(epsD[:], 1e-5 * D), "wi", 1)
    for dst, src in [(s_i, s_i_h), (s_f, s_f_h), (s_g, s_g_h), (s_o, s_o_h),
                     (gnw, gnw_h), (ds, ds_h), (gate_s, gs_h), (up_s, us_h),
                     (cmask, cmask_h)]:
        B.op("g", lambda e, S, dst=dst, src=src: e.dma_start(out=dst[:], in_=src),
             "wi", 16)
    wd_init = B.cnt["wi"]
    B.wait("v", "wi", wd_init)
    for t_ in (s_i, s_f, s_g):
        B.op("v", lambda e, S, t_=t_: e.tensor_scalar_mul(t_[:], t_[:], 1.0 / 256.0),
             "ev")
    B.op("v", lambda e, S: e.tensor_mul(gnwso[:], gnw[:], s_o[:]), "ev")
    ev_init = B.op("v", lambda e, S: e.tensor_scalar_mul(dsh[:], ds[:], 0.5), "ev")

    _xsems = ["xa", "xb", "xc", "xd"]
    x_ready = []
    for grp in range(4):
        v = B.op("y", lambda e, S, grp=grp: e.dma_start(
            out=xF[:, grp * 4 * T:(grp + 1) * 4 * T],
            in_=x0[:, grp * 4 * T:(grp + 1) * 4 * T]), _xsems[grp], 16)
        x_ready += [(_xsems[grp], v)] * 4

    def ln_state():
        return {"sc_sq": [0] * KC, "mm_sq": [0] * KC, "ev_cst": [0] * KC}

    def emit_ln_chunk_sv(st, k, x_rdy, xn_clear=0):
        base = (k % 2) * 2 * T
        B.wait("s", x_rdy[0], x_rdy[1])
        B.wait("v", x_rdy[0], x_rdy[1])
        if k == 0 and xn_clear:
            B.wait("v", "mm", xn_clear)
        if k >= 2:
            B.wait("s", "mm", st["mm_sq"][k - 2])
        st["sc_sq"][k] = B.op("s", lambda e, S, k=k, base=base: e.activation(
            lnx[:, base + T:base + 2 * T], xs(k), ACT.Square), "sc")
        st["ev_cst"][k] = B.op("v", lambda e, S, k=k: e.tensor_copy(
            xn_s[:, k * T:(k + 1) * T], xs(k)), "ev")

    def emit_ln_chunk_t(st, k, rp):
        base = (k % 2) * 2 * T
        if k == 0:
            use_psum(rp[0])
            use_psum(rp[1])
        B.wait("t", "ev", st["ev_cst"][k])
        B.wait("t", "sc", st["sc_sq"][k])
        for th in range(TH):
            B.op("t", lambda e, S, k=k, th=th, rp=rp: e.matmul(
                p[rp[0]][0:1, th * 512:(th + 1) * 512], ones_rb[:],
                xnth(k, th), start=(k == 0), stop=(k == KC - 1),
                tile_position=(0, 0)))
            last = (th == TH - 1)
            fn = lambda e, S, k=k, th=th, base=base, rp=rp: e.matmul(
                p[rp[1]][32:33, th * 512:(th + 1) * 512], ones_rb[:],
                lnx[:, base + T + th * 512: base + T + (th + 1) * 512],
                start=(k == 0), stop=(k == KC - 1), tile_position=(0, 32))
            if last:
                st["mm_sq"][k] = B.op("t", fn, "mm")
            else:
                B.op("t", fn)

    def emit_ln_chunk(st, k, x_rdy, rp):
        emit_ln_chunk_sv(st, k, x_rdy)
        emit_ln_chunk_t(st, k, rp)

    def emit_ln_tail(st, fix, xn_wclear_mm, rp, bp):
        mm_red = st["mm_sq"][KC - 1]
        B.wait("v", "mm", mm_red)
        B.wait("s", "mm", mm_red)
        sc_m = B.op("s", lambda e, S, rp=rp: e.copy(r1b, p[rp[0]][0:1, :]), "sc")
        B.wait("v", "sc", sc_m)
        B.op("v", lambda e, S: e.tensor_mul(r2[:], r1b, r1b), "ev")
        ev_var = B.op("v", lambda e, S, rp=rp: e.scalar_tensor_tensor(
            r2[:], r2[:], -1.0 / D, p[rp[1]][32:33, :], ALU.mult, ALU.add), "ev")
        set_pcons(rp[0], "ev", ev_var)
        set_pcons(rp[1], "ev", ev_var)
        B.wait("s", "ev", ev_var)
        B.op("s", lambda e, S: e.activation(
            r2[:], r2[:], ACT.Ln, bias=epsD[0:1, :], scale=1.0), "sc")
        sc_exp = B.op("s", lambda e, S: e.activation(
            r2b, r2[:], ACT.Exp, bias=0.0, scale=-0.5), "sc")
        ev_rows = None
        for kb_sem, kb_val in (("sc", sc_m), ("ev", ev_var), ("sc", sc_exp)):
            B.wait("t", kb_sem, kb_val)
            B.op("t", lambda e, S, rp=rp: e.matmul(
                p[rp[0]][64:65, 0:64], ones_rb[:], lnx[:, 0:64],
                start=True, stop=True, tile_position=(0, 64)))
        B.wait("t", "sc", sc_exp)
        use_psum(bp[0])
        use_psum(bp[1])
        invrow = sqD256_b if fix else sqD_b
        for th in range(TH):
            B.op("t", lambda e, S, th=th, bp=bp, invrow=invrow: e.matmul(
                p[bp[0]][:, th * 512:(th + 1) * 512], invrow[:],
                r2b[0:1, th * 512:(th + 1) * 512], start=True, stop=True))
        mm_bc = None
        for th in range(TH):
            mm_bc = B.op("t", lambda e, S, th=th, bp=bp: e.matmul(
                p[bp[1]][:, th * 512:(th + 1) * 512], invD_b[:],
                r1b[0:1, th * 512:(th + 1) * 512], start=True, stop=True), "mm")
        B.wait("s", "mm", mm_bc)
        B.op("s", lambda e, S, bp=bp: e.copy(invB[:], p[bp[0]][:]), "sc")
        sc_bc = B.op("s", lambda e, S, bp=bp: e.copy(minvB[:], p[bp[1]][:]), "sc")
        set_pcons(bp[0], "sc", sc_bc)
        set_pcons(bp[1], "sc", sc_bc)
        B.wait("v", "sc", sc_bc)
        ev_ap = sc_cp = 0
        rdy = []
        sc_cp_k = [0] * KC
        for k in range(KC):
            xk = xn_s[:, k * T:(k + 1) * T]
            B.op("v", lambda e, S, xk=xk: e.tensor_sub(xk, xk, minvB[:]), "ev")
            if not fix:
                ev_ap = B.op("v", lambda e, S, xk=xk: e.tensor_mul(
                    xk, xk, invB[:]), "ev")
                rdy.append(("ev", ev_ap))
            else:
                if k >= 2:
                    B.wait("v", "sc", sc_cp_k[k - 2])
                ev_ap = B.op("v", lambda e, S, k=k, xk=xk: e.tensor_mul(
                    iQ[k % 2], xk, invB[:]), "ev")
                B.wait("s", "ev", ev_ap)
                val = B.op("s", lambda e, S, k=k: e.copy(
                    xn_s[:, k * T:(k + 1) * T], iQ[k % 2]), "sc")
                sc_cp_k[k] = val
                sc_cp = val
                rdy.append(("sc", val))
        xn_last = ("sc", sc_cp) if fix else rdy[-1]
        return xn_last, rdy

    def emit_ffn_prefetch():
        vals = {}
        for s_id, srcw, blk in ((0, gw_b, 0), (1, uw_b, 0), (2, gw_b, 1),
                                (3, uw_b, 1)):
            sem, val = wcons[s_id]
            B.wait("g", sem, val)
            vals[s_id] = B.op("g", lambda e, S, s_id=s_id, srcw=srcw, blk=blk:
                              e.dma_start(out=slot(s_id, 2048), in_=srcw[blk]),
                              "w%d" % s_id, 16)
        return vals

    def emit_ffn(xn_ready, post_cb=None, pre=None):
        xn_last, xn_rdy = xn_ready
        resid_ev = 0
        resid_list = [0] * KC
        for ih in range(IH):
            sc_silu = {}
            hdn_ev = {}
            mm_gate = {}
            mm_up = {}
            for idx in range(ICH):
                ic = ih * ICH + idx
                par = idx % 2
                sg, su = 2 * par, 2 * par + 1
                if pre is not None and ih == 0 and idx < 2:
                    wd_g, wd_u = pre[sg], pre[su]
                else:
                    for s_id, src in ((sg, gw_b), (su, uw_b)):
                        sem, val = wcons[s_id]
                        B.wait("g", sem, val)
                        wdv = B.op("g", lambda e, S, s_id=s_id, src=src, ic=ic:
                                   e.dma_start(out=slot(s_id, 2048), in_=src[ic]),
                                   "w%d" % s_id, 16)
                        if s_id == sg:
                            wd_g = wdv
                        else:
                            wd_u = wdv
                B.wait("t", "w%d" % sg, wd_g)
                ladder = (idx == 0 and ih == 0)
                pg = p[0 if par == 0 else 2]
                pu = p[1 if par == 0 else 3]
                sem, val = pcons[0 if par == 0 else 2]
                B.wait("t", sem, val)
                for kc in range(KC):
                    if ladder:
                        B.wait("t", xn_rdy[kc][0], xn_rdy[kc][1])
                    for th in range(TH):
                        last = (th == TH - 1 and kc == KC - 1)
                        fn = lambda e, S, th=th, kc=kc, sg=sg, pg=pg: e.matmul(
                            pg[:, th * 512:(th + 1) * 512],
                            slot(sg, 2048)[:, kc * 128:(kc + 1) * 128],
                            xnth(kc, th), start=(kc == 0), stop=(kc == KC - 1))
                        if last:
                            mm_gate[idx] = B.op("t", fn, "mm")
                        else:
                            B.op("t", fn)
                wcons[sg] = ("mm", mm_gate[idx])
                B.wait("t", "w%d" % su, wd_u)
                sem, val = pcons[1 if par == 0 else 3]
                B.wait("t", sem, val)
                for kc in range(KC):
                    for th in range(TH):
                        last = (th == TH - 1 and kc == KC - 1)
                        fn = lambda e, S, th=th, kc=kc, su=su, pu=pu: e.matmul(
                            pu[:, th * 512:(th + 1) * 512],
                            slot(su, 2048)[:, kc * 128:(kc + 1) * 128],
                            xnth(kc, th), start=(kc == 0), stop=(kc == KC - 1))
                        if last:
                            mm_up[idx] = B.op("t", fn, "mm")
                        else:
                            B.op("t", fn)
                wcons[su] = ("mm", mm_up[idx])
                B.wait("s", "mm", mm_gate[idx])
                if idx >= 2:
                    B.wait("s", "ev", hdn_ev[idx - 2])
                if ih == 0 and idx < 2:
                    B.wait("s", "ev", ev_init)
                sc_silu[idx] = B.op("s", lambda e, S, par=par, ic=ic, pg=pg:
                                    e.activation(sqbuf[par][:], pg[:], ACT.Silu,
                                                 scale=gate_s[:, ic:ic + 1]), "sc")
                pcons[0 if par == 0 else 2] = ("sc", sc_silu[idx])
                B.wait("v", "mm", mm_up[idx])
                B.wait("v", "sc", sc_silu[idx])
                hdn_ev[idx] = B.op("v", lambda e, S, idx=idx, ic=ic, par=par, pu=pu:
                                   e.scalar_tensor_tensor(
                                       hd_s[:, idx * T:(idx + 1) * T], pu[:],
                                       up_s[:, ic:ic + 1], sqbuf[par][:],
                                       ALU.mult, ALU.mult), "ev")
                pcons[1 if par == 0 else 3] = ("ev", hdn_ev[idx])
            for oc in range(KC):
                par = oc % 2
                sd = 0 if par == 0 else 2
                sem, val = wcons[sd]
                B.wait("g", sem, val)
                wd_d = B.op("g", lambda e, S, sd=sd, oc=oc, ih=ih: e.dma_start(
                    out=slot(sd), in_=dw_b[oc, ih]), "w%d" % sd, 16)
                B.wait("t", "w%d" % sd, wd_d)
                if oc == 0:
                    B.wait("t", "ev", hdn_ev[ICH - 1])
                pd = p[0 if par == 0 else 2]
                sem, val = pcons[0 if par == 0 else 2]
                B.wait("t", sem, val)
                for kc in range(ICH):
                    for th in range(TH):
                        last = (th == TH - 1 and kc == ICH - 1)
                        fn = lambda e, S, th=th, kc=kc, sd=sd, pd=pd: e.matmul(
                            pd[:, th * 512:(th + 1) * 512],
                            slot(sd)[:, kc * 128:(kc + 1) * 128],
                            hdth(kc, th), start=(kc == 0), stop=(kc == ICH - 1))
                        if last:
                            mm_d = B.op("t", fn, "mm")
                        else:
                            B.op("t", fn)
                wcons[sd] = ("mm", mm_d)
                B.wait("v", "mm", mm_d)
                resid_ev = B.op("v", lambda e, S, oc=oc, pd=pd:
                                e.scalar_tensor_tensor(
                                    xs(oc), pd[:], dsh[:, oc:oc + 1], xs(oc),
                                    ALU.mult, ALU.add), "ev")
                pcons[0 if par == 0 else 2] = ("ev", resid_ev)
                resid_list[oc] = resid_ev
                if post_cb is not None and ih == IH - 1:
                    post_cb(oc, resid_ev, xn_clear=mm_up[ICH - 1])
        return resid_ev, resid_list

    def lagged_ln_cb(st, rp):
        pend = []

        def cb(oc, ev, xn_clear=0):
            if len(pend) >= 2:
                emit_ln_chunk_t(st, pend.pop(0), rp)
            emit_ln_chunk_sv(st, oc, ("ev", ev), xn_clear)
            pend.append(oc)

        def flush():
            while pend:
                emit_ln_chunk_t(st, pend.pop(0), rp)
        return cb, flush

    ln1 = ln_state()
    for k in range(KC):
        if k >= 1:
            emit_ln_chunk_t(ln1, k - 1, (1, 3))
        emit_ln_chunk_sv(ln1, k, x_ready[k])
    emit_ln_chunk_t(ln1, KC - 1, (1, 3))
    ffn1_pre = emit_ffn_prefetch()
    xn_last1, xn_rdy1 = emit_ln_tail(ln1, fix=False, xn_wclear_mm=0,
                                     rp=(1, 3), bp=(0, 2))
    ln2 = ln_state()
    ln2_cb, ln2_flush = lagged_ln_cb(ln2, (1, 3))
    x1_ev, x1_list = emit_ffn((xn_last1, xn_rdy1), post_cb=ln2_cb, pre=ffn1_pre)
    ln2_flush()
    last_xn_mm = B.cnt["mm"]
    attn_pre = {}
    for s_id, srcw, blk in ((0, wf_b, 0), (1, wi_b, 0), (2, wf_b, 1), (3, wi_b, 1)):
        sem, val = wcons[s_id]
        B.wait("g", sem, val)
        attn_pre[s_id] = B.op("g", lambda e, S, s_id=s_id, srcw=srcw, blk=blk:
                              e.dma_start(out=slot(s_id, 2048), in_=srcw[blk]),
                              "w%d" % s_id, 16)
    xn2_last, xn2_rdy = emit_ln_tail(ln2, fix=True, xn_wclear_mm=last_xn_mm,
                                     rp=(1, 3), bp=(0, 2))

    B.wait("y", "ev", B.cnt["ev"])
    xd_spill = B.op("y", lambda e, S: e.dma_start(out=x1_dram.ap(), in_=xF), "xs", 16)

    wg_pre_buf = [hd_s[:, (16 + 2 * j) * T:(16 + 2 * j + 2) * T] for j in range(3)]
    wg_pre_buf += [sq[:].bitcast(BF16), sc2[:].bitcast(BF16), sc1[:].bitcast(BF16)]
    _wg_sems = ["xe", "xf", "xg", "xh", "gv", "wi"]
    wg_pre_wd = [0] * 6
    for j in range(5):
        B.wait("g", "ev", x1_ev)
        if j >= 3:
            B.wait("g", xn2_last[0], xn2_last[1])
        wg_pre_wd[j] = B.op("g", lambda e, S, j=j: e.dma_start(
            out=wg_pre_buf[j], in_=wg_b[j]), _wg_sems[j], 16)
    scan_ev = [0] * KC
    sc_f = [0] * KC
    mm_f = [0] * KC
    mm_i = [0] * KC
    for ec in range(KC):
        par = ec % 2
        sfslot, sislot = 2 * par, 2 * par + 1
        if ec < 2:
            wd_f, wd_i = attn_pre[sfslot], attn_pre[sislot]
        else:
            for s_id, src in ((sfslot, wf_b), (sislot, wi_b)):
                sem, val = wcons[s_id]
                B.wait("g", sem, val)
                wdv = B.op("g", lambda e, S, s_id=s_id, src=src, ec=ec: e.dma_start(
                    out=slot(s_id, 2048), in_=src[ec]), "w%d" % s_id, 16)
                if s_id == sfslot:
                    wd_f = wdv
                else:
                    wd_i = wdv
        B.wait("t", "w%d" % sfslot, wd_f)
        pf = p[0 if par == 0 else 2]
        pi = p[1 if par == 0 else 3]
        sem, val = pcons[0 if par == 0 else 2]
        B.wait("t", sem, val)
        for kc in range(KC):
            if ec == 0:
                B.wait("t", xn2_rdy[kc][0], xn2_rdy[kc][1])
            for th in range(TH):
                last = (th == TH - 1 and kc == KC - 1)
                fn = lambda e, S, th=th, kc=kc, sfslot=sfslot, pf=pf: e.matmul(
                    pf[:, th * 512:(th + 1) * 512],
                    slot(sfslot, 2048)[:, kc * 128:(kc + 1) * 128],
                    xnth(kc, th), start=(kc == 0), stop=(kc == KC - 1))
                if last:
                    mm_f[ec] = B.op("t", fn, "mm")
                else:
                    B.op("t", fn)
        wcons[sfslot] = ("mm", mm_f[ec])
        B.wait("t", "w%d" % sislot, wd_i)
        sem, val = pcons[1 if par == 0 else 3]
        B.wait("t", sem, val)
        for kc in range(KC):
            for th in range(TH):
                last = (th == TH - 1 and kc == KC - 1)
                fn = lambda e, S, th=th, kc=kc, sislot=sislot, pi=pi: e.matmul(
                    pi[:, th * 512:(th + 1) * 512],
                    slot(sislot, 2048)[:, kc * 128:(kc + 1) * 128],
                    xnth(kc, th), start=(kc == 0), stop=(kc == KC - 1))
                if last:
                    mm_i[ec] = B.op("t", fn, "mm")
                else:
                    B.op("t", fn)
        wcons[sislot] = ("mm", mm_i[ec])
        B.wait("s", "mm", mm_f[ec])
        if ec == 0:
            B.wait("s", "xs", xd_spill)
        if ec >= 2:
            B.wait("s", "ev", scan_ev[ec - 2])
        sc_f[ec] = B.op("s", lambda e, S, ec=ec, par=par, pf=pf: e.activation(
            f_sb[par], pf[:], ACT.Sigmoid, scale=s_f[:, ec:ec + 1]), "sc")
        pcons[0 if par == 0 else 2] = ("sc", sc_f[ec])
        B.wait("v", "mm", mm_i[ec])
        B.wait("v", "sc", sc_f[ec])
        if ec == 0:
            B.wait("v", "xs", xd_spill)
        B.op("v", lambda e, S, ec=ec, par=par, pi=pi: e.scalar_tensor_tensor(
            sc1[:], pi[:], s_i[:, ec:ec + 1], f_sb[par], ALU.mult, ALU.mult), "ev")
        ev_i = B.op("v", lambda e, S, ec=ec, par=par, pi=pi: e.scalar_tensor_tensor(
            i_sb[par], pi[:], s_i[:, ec:ec + 1], sc1[:], ALU.mult, ALU.subtract),
            "ev")
        pcons[1 if par == 0 else 3] = ("ev", ev_i)
        B.op("v", lambda e, S, ec=ec, par=par: e.tensor_tensor_scan(
            hv(ec), f_sb[par], i_sb[par], 0.0, ALU.mult, ALU.add), "ev")
        B.op("v", lambda e, S, ec=ec, par=par: e.tensor_tensor_scan(
            Pv(ec), f_sb[par], f_sb[par], 1.0, ALU.mult, ALU.bypass), "ev")
        scan_ev[ec] = B.op("v", lambda e, S, ec=ec: e.tensor_copy(
            hend[:, ec:ec + 1], hv(ec)[:, T - 1:T]), "ev")

    wd_g2 = [0] * KC
    B.wait("g", "ev", scan_ev[KC - 1])
    wg_pre_wd[5] = B.op("g", lambda e, S: e.dma_start(
        out=wg_pre_buf[5], in_=wg_b[5]), _wg_sems[5], 16)
    wd_ci = B.cnt["wi"] + 16
    wd_carry = B.cnt["wi"] + 32
    _cw = {0: B.cnt["w0"], 2: B.cnt["w2"]}
    for ec in range(6, KC):
        sgslot = 2 * (ec % 2)
        _cw[sgslot] += 16
        wd_g2[ec] = _cw[sgslot]
    B.wait("v", "wi", wd_carry)
    B.op("v", lambda e, S: e.tensor_scalar_mul(carry[:], carry[:], cmask[:, 0:1]),
         "ev")
    fix_ev = [0] * KC
    for ec in range(KC):
        fix_ev[ec] = B.op("v", lambda e, S, ec=ec: e.scalar_tensor_tensor(
            hv(ec), Pv(ec), carry[:, ec:ec + 1], hv(ec), ALU.mult, ALU.add), "ev")

    def sg_store(ec):
        return Pb[:, 16384 + ec * T: 16384 + (ec + 1) * T]

    sq2_ev = [0] * KC
    mm_sq2 = [0] * KC
    sc_silu2 = [0] * KC
    u2_ev = [0] * KC
    x1b_bf = x1buf[:].bitcast(BF16)
    sqst = [lnx[:, j * T:(j + 1) * T] for j in range(4)] + \
           [x1b_bf[:, j * T:(j + 1) * T] for j in range(4)]

    def emit_rms_red(ec):
        if ec == 0:
            sem, val = pcons[3]
            B.wait("t", sem, val)
        B.wait("t", "ev", sq2_ev[ec])
        for th in range(TH):
            last = (th == TH - 1)
            fn = lambda e, S, ec=ec, th=th: e.matmul(
                p[3][0:1, th * 512:(th + 1) * 512], ones_rb[:],
                sqst[ec % 8][:, th * 512:(th + 1) * 512],
                start=(ec == 0), stop=(ec == KC - 1))
            if last:
                mm_sq2[ec] = B.op("t", fn, "mm")
            else:
                B.op("t", fn)

    def emit_sq_u2(k):
        if k >= 8:
            B.wait("v", "mm", mm_sq2[k - 8])
        sq2_ev[k] = B.op("v", lambda e, S, k=k: e.tensor_mul(
            sqst[k % 8], hv(k), hv(k)), "ev")
        B.wait("v", "sc", sc_silu2[k])
        u2_ev[k] = B.op("v", lambda e, S, k=k: e.scalar_tensor_tensor(
            hv(k), hv(k), gnwso[:, k:k + 1], sg_store(k), ALU.mult, ALU.mult), "ev")

    mm_g_list = [0] * KC
    for ec in range(KC):
        par = ec % 2
        sgslot = 2 * par
        if ec < 6:
            B.wait("t", _wg_sems[ec], wg_pre_wd[ec])
            wsrc = wg_pre_buf[ec]
        else:
            B.wait("t", "w%d" % sgslot, wd_g2[ec])
            wsrc = slot(sgslot, 2048)
        pg = p[0 if par == 0 else 2]
        sem, val = pcons[0 if par == 0 else 2]
        B.wait("t", sem, val)
        for kc in range(KC):
            for th in range(TH):
                last = (th == TH - 1 and kc == KC - 1)
                fn = lambda e, S, th=th, kc=kc, wsrc=wsrc, pg=pg: e.matmul(
                    pg[:, th * 512:(th + 1) * 512],
                    wsrc[:, kc * 128:(kc + 1) * 128],
                    xnth(kc, th), start=(kc == 0), stop=(kc == KC - 1))
                if last:
                    mm_g = B.op("t", fn, "mm")
                else:
                    B.op("t", fn)
        mm_g_list[ec] = mm_g
        if ec >= 8:
            emit_rms_red(ec - 8)
        B.wait("s", "mm", mm_g)
        if ec == 0:
            B.wait("s", "ev", scan_ev[KC - 1])
        sc_silu2[ec] = B.op("s", lambda e, S, ec=ec, pg=pg: e.activation(
            sg_store(ec), pg[:], ACT.Silu, scale=s_g[:, ec:ec + 1]), "sc")
        pcons[0 if par == 0 else 2] = ("sc", sc_silu2[ec])
        if ec >= 2:
            emit_sq_u2(ec - 2)
    for k in range(KC - 2, KC):
        emit_sq_u2(k)
    for ec in range(KC - 8, KC):
        emit_rms_red(ec)
    v = B.op("g", lambda e, S: e.dma_start(out=cc_in.ap(), in_=hend[:]), "wi", 16)
    assert v == wd_ci
    B.wait("g", "wi", wd_ci)
    cc_v = B.op("g", lambda e, S: e.collective_compute(
        "AllGather", ALU.bypass, replica_groups=RG,
        ins=[cc_in.ap().opt()], outs=[cc_out.ap().opt()]), "cc", 1)

    def emit_wg_dma(ec):
        sgslot = 2 * (ec % 2)
        if ec in (6, 7):
            sem, val = wcons[sgslot]
        else:
            sem, val = "mm", mm_g_list[ec - 2]
        B.wait("g", sem, val)
        v = B.op("g", lambda e, S, sgslot=sgslot, ec=ec: e.dma_start(
            out=slot(sgslot, 2048), in_=wg_b[ec]), "w%d" % sgslot, 16)
        assert v == wd_g2[ec], (ec, v, wd_g2[ec])

    emit_wg_dma(6)
    emit_wg_dma(7)
    B.wait("g", "cc", cc_v)
    v = B.op("g", lambda e, S: e.dma_start(
        out=carry[:], in_=cc_out.ap()[0:128, :]), "wi", 16)
    assert v == wd_carry
    for ec in range(8, KC):
        emit_wg_dma(ec)
    wcons[0] = ("mm", mm_g_list[KC - 2])
    wcons[2] = ("mm", mm_g_list[KC - 1])
    mm_g_last = B.cnt["mm"]
    mm_rms = mm_sq2[KC - 1]
    set_pcons(3, "mm", mm_rms)
    B.wait("s", "mm", mm_rms)
    sc_rs = B.op("s", lambda e, S: e.activation(
        r2[:], p[3][0:1, :], ACT.Ln, bias=epsD[0:1, :], scale=1.0), "sc")
    set_pcons(3, "sc", sc_rs)
    sc_re = B.op("s", lambda e, S: e.activation(
        r2b, r2[:], ACT.Exp, bias=0.0, scale=-0.5), "sc")
    for kb_sem, kb_val in (("sc", sc_rs), ("sc", sc_re)):
        B.wait("t", kb_sem, kb_val)
        B.op("t", lambda e, S: e.matmul(
            p[3][64:65, 0:64], ones_rb[:], lnx[:, 0:64],
            start=True, stop=True, tile_position=(0, 64)))
    B.wait("t", "sc", sc_re)
    sem, val = pcons[1]
    B.wait("t", sem, val)
    mm_rb = None
    for th in range(TH):
        mm_rb = B.op("t", lambda e, S, th=th: e.matmul(
            p[1][:, th * 512:(th + 1) * 512], sqD_b[:],
            r2b[0:1, th * 512:(th + 1) * 512], start=True, stop=True), "mm")
    B.wait("s", "mm", mm_rb)
    sc_rb = B.op("s", lambda e, S: e.copy(invB[:], p[1][:]), "sc")
    set_pcons(1, "sc", sc_rb)
    o_ev = [0] * KC
    B.wait("v", "sc", sc_rb)
    for ec in range(KC):
        o_ev[ec] = B.op("v", lambda e, S, ec=ec: e.tensor_mul(
            hv(ec), hv(ec), invB[:]), "ev")

    ln3 = ln_state()
    ln3_cb, ln3_flush = lagged_ln_cb(ln3, (0, 2))
    x1cons = [0, 0]
    x2_ev = [0] * KC
    for oc in range(KC):
        par = oc % 2
        soslot = 2 * par + 1
        sem, val = wcons[soslot]
        B.wait("g", sem, val)
        wd_o = B.op("g", lambda e, S, soslot=soslot, oc=oc: e.dma_start(
            out=slot(soslot, 2048), in_=wo_b[oc]), "w%d" % soslot, 16)
        if oc < 2:
            B.wait("y", "mm", mm_sq2[KC - 1])
        B.wait("y", "ev", x1cons[par])
        xsem = "xa" if par == 0 else "xb"
        xd_x1 = B.op("y", lambda e, S, oc=oc, par=par: e.dma_start(
            out=x1buf[:, par * T:(par + 1) * T],
            in_=x1_dram.ap()[:, oc * T:(oc + 1) * T]), xsem, 16)
        B.wait("t", "w%d" % soslot, wd_o)
        po = p[1 if par == 0 else 3]
        sem, val = pcons[1 if par == 0 else 3]
        B.wait("t", sem, val)
        for kc in range(KC):
            if oc == 0:
                B.wait("t", "ev", o_ev[kc])
            for th in range(TH):
                last = (th == TH - 1 and kc == KC - 1)
                fn = lambda e, S, th=th, kc=kc, soslot=soslot, po=po: e.matmul(
                    po[:, th * 512:(th + 1) * 512],
                    slot(soslot, 2048)[:, kc * 128:(kc + 1) * 128],
                    hdth(kc, th), start=(kc == 0), stop=(kc == KC - 1))
                if last:
                    mm_o = B.op("t", fn, "mm")
                else:
                    B.op("t", fn)
        wcons[soslot] = ("mm", mm_o)
        B.wait("v", "mm", mm_o)
        B.wait("v", xsem, xd_x1)
        B.op("v", lambda e, S, oc=oc, par=par, po=po: e.tensor_scalar_mul(
            iQ32[par][:, 0:T], po[:], 256.0), "ev")
        x2_ev[oc] = B.op("v", lambda e, S, oc=oc, par=par: e.scalar_tensor_tensor(
            xs(oc), iQ32[par][:, 0:T], 1.0 / 256.0, x1buf[:, par * T:(par + 1) * T],
            ALU.mult, ALU.add), "ev")
        pcons[1 if par == 0 else 3] = ("ev", x2_ev[oc])
        x1cons[par] = x2_ev[oc]
        ln3_cb(oc, x2_ev[oc], xn_clear=mm_g_last)

    last_xn_mm2 = B.cnt["mm"]
    ln3_flush()
    ffn2_pre = emit_ffn_prefetch()
    xn3_last, xn3_rdy = emit_ln_tail(ln3, fix=False, xn_wclear_mm=last_xn_mm2,
                                     rp=(0, 2), bp=(1, 3))
    x3_ev, x3_list = emit_ffn((xn3_last, xn3_rdy), pre=ffn2_pre)

    xd_out = 0
    for oc in range(KC):
        B.wait("y", "ev", x3_list[oc])
        xd_out = B.op("y", lambda e, S, oc=oc: e.dma_start(
            out=y[:, oc * T:(oc + 1) * T], in_=xs(oc)), "xs", 16)
    B.wait("y", "xs", xd_out)

    from contextlib import ExitStack as _ES
    sem_ctx = _ES()
    S = {}
    for nm in ["mm", "sc", "ev", "cc", "gv", "wi", "w0", "w1", "w2", "w3",
               "xa", "xb", "xc", "xd", "xe", "xf", "xg", "xh", "xs"]:
        S[nm] = sem_ctx.enter_context(nc.semaphore("s_" + nm))
    with nc.Block() as block:

        @block.gpsimd
        def _(e):
            for fn in B.q["g"]:
                fn(e, S)

        @block.sync
        def _(e):
            for fn in B.q["y"]:
                fn(e, S)

        @block.tensor
        def _(e):
            for fn in B.q["t"]:
                fn(e, S)

        @block.scalar
        def _(e):
            for fn in B.q["s"]:
                fn(e, S)

        @block.vector
        def _(e):
            for fn in B.q["v"]:
                fn(e, S)

    sem_ctx.close()
    ctx.close()
    return nc


def _chunked(v):
    n = v.shape[0] // 128
    return np.ascontiguousarray(v.reshape(n, 128).T)


def kernel(L0_attn_hidden_states, L0_attn_w_i, L0_attn_w_f, L0_attn_w_g,
           L0_attn_w_o, L0_attn_s_i, L0_attn_s_f, L0_attn_s_g, L0_attn_s_o,
           L0_attn_g_norm_w, L0_ffn_gate_w, L0_ffn_up_w, L0_ffn_down_w,
           L0_ffn_gate_s, L0_ffn_up_s, L0_ffn_down_s):
    f32 = np.float32
    x = np.asarray(L0_attn_hidden_states, f32)

    def blk_dxd(w):
        return np.ascontiguousarray(
            np.asarray(w, f32).reshape(KC, 128, KC, 128).transpose(2, 1, 0, 3)
            .reshape(KC, 128, 2048))

    wi_b = blk_dxd(L0_attn_w_i)
    wf_b = blk_dxd(L0_attn_w_f)
    wg_b = blk_dxd(L0_attn_w_g)
    wo_b = blk_dxd(np.asarray(L0_attn_w_o, f32).T)
    gw_b = np.ascontiguousarray(
        np.asarray(L0_ffn_gate_w, f32).T.reshape(KC, 128, ICN, 128)
        .transpose(2, 1, 0, 3).reshape(ICN, 128, 2048))
    uw_b = np.ascontiguousarray(
        np.asarray(L0_ffn_up_w, f32).T.reshape(KC, 128, ICN, 128)
        .transpose(2, 1, 0, 3).reshape(ICN, 128, 2048))
    dw_b = np.ascontiguousarray(
        np.asarray(L0_ffn_down_w, f32).T.reshape(IH, ICH, 128, KC, 128)
        .transpose(3, 0, 2, 1, 4).reshape(KC, IH, 128, SLOT))

    base = {
        "wi_b": wi_b, "wf_b": wf_b, "wg_b": wg_b, "wo_b": wo_b,
        "gw_b": gw_b, "uw_b": uw_b, "dw_b": dw_b,
        "s_i": _chunked(np.asarray(L0_attn_s_i, f32)),
        "s_f": _chunked(np.asarray(L0_attn_s_f, f32)),
        "s_g": _chunked(np.asarray(L0_attn_s_g, f32)),
        "s_o": _chunked(np.asarray(L0_attn_s_o, f32)),
        "gnw": _chunked(np.asarray(L0_attn_g_norm_w, f32)),
        "ds": _chunked(np.asarray(L0_ffn_down_s, f32)),
        "gs": _chunked(np.asarray(L0_ffn_gate_s, f32)),
        "us": _chunked(np.asarray(L0_ffn_up_s, f32)),
    }
    in_maps = []
    for c in range(8):
        b, h = c // 2, c % 2
        m = dict(base)
        m["x0"] = np.ascontiguousarray(
            x[b, h * T:(h + 1) * T, :].T.reshape(KC, 128, T)
            .transpose(1, 0, 2)).reshape(128, KC * T)
        m["cmask"] = np.full((128, 1), float(h), f32)
        in_maps.append(m)

    nc = build_nc()
    res = run_bass_kernel_spmd(nc, in_maps, core_ids=list(range(8)))
    out = np.empty((4, 2048, D), f32)
    for c in range(8):
        b, h = c // 2, c % 2
        yfm = res.results[c]["y"].reshape(128, KC, T).transpose(1, 0, 2)
        out[b, h * T:(h + 1) * T, :] = yfm.reshape(D, T).T
    return out
